# version 05
# baseline (speedup 1.0000x reference)
import contextlib
from collections import defaultdict

import numpy as np
import concourse.bass as bass
import concourse.mybir as mybir
from concourse.bass_utils import run_bass_kernel_spmd

F32 = mybir.dt.float32
BF16 = mybir.dt.bfloat16
AF = mybir.ActivationFunctionType
ALU = mybir.AluOpType
AX = mybir.AxisListType

NCORE = 8
H = 2048
HT = 16
IT = 32
E = 8
IET = 11
FT = 88
ST = 22
DKT = FT + ST
TC = 512
NCH = 4
TCORE = TC * NCH
NBUF = 6
PCOLS = 2048
KB = 16
EPS = 1e-5
INV_H = 1.0 / H
DOWN_BLOCKS = [(16 * i, 16) for i in range(6)] + [(96, 14)]

STAGE = 2
DEBUG_DUMP = False


def build_program():
    nc = bass.Bass("TRN2", target_bir_lowering=False, debug=False)

    x0t_h = nc.dram_tensor("x0t", [NCH, 128, HT * TC], F32, kind="ExternalInput").ap()
    out_h = nc.dram_tensor("out", [NCH, 128, HT * TC], F32, kind="ExternalOutput").ap()
    wh = {}
    for name, npan, cols in [
        ("win", 64, PCOLS), ("conv", 64, PCOLS), ("wout", 32, PCOLS),
        ("w1", 88, PCOLS), ("w2", 88, PCOLS), ("sw1", 22, PCOLS), ("sw2", 22, PCOLS),
        ("d16", 96, PCOLS), ("d14", 16, 14 * 128),
    ]:
        wh[name] = nc.dram_tensor(name, [npan, 128, cols], F32, kind="ExternalInput").ap()
    gate_h = nc.dram_tensor("gate", [128, 128], F32, kind="ExternalInput").ap()
    dvec_h = nc.dram_tensor("dvec", [128, 32], F32, kind="ExternalInput").ap()
    ident_h = nc.dram_tensor("ident", [128, 128], F32, kind="ExternalInput").ap()
    onescol_h = nc.dram_tensor("onescol", [128, 2], F32, kind="ExternalInput").ap()
    onesrow_h = nc.dram_tensor("onesrow", [1, 128], F32, kind="ExternalInput").ap()
    sel_h = nc.dram_tensor("sel", [8, 1024], F32, kind="ExternalInput").ap()
    ones1_h = nc.dram_tensor("ones1", [128, 1], F32, kind="ExternalInput").ap()
    if DEBUG_DUMP:
        dbgu_h = nc.dram_tensor("dbgu", [NCH, 128, DKT * TC], BF16, kind="ExternalOutput").ap()
        dbgl_h = nc.dram_tensor("dbgl", [NCH, 128, HT * TC], BF16, kind="ExternalOutput").ap()
        dbgc_h = nc.dram_tensor("dbgc", [NCH, 8, TC], BF16, kind="ExternalOutput").ap()
        dbgg_h = nc.dram_tensor("dbgg", [NCH, 128, 320], F32, kind="ExternalOutput").ap()
        dbgs_h = nc.dram_tensor("dbgs", [NCH, 8, TC], F32, kind="ExternalOutput").ap()

    with contextlib.ExitStack() as stack:
        Ecm = stack.enter_context
        x0 = Ecm(nc.sbuf_tensor("x0", [128, HT * TC], F32))
        lnb = Ecm(nc.sbuf_tensor("lnb", [128, HT * TC], BF16))
        ub = Ecm(nc.sbuf_tensor("ub", [128, DKT * TC], BF16))
        wb = [Ecm(nc.sbuf_tensor(f"wb{i}", [128, PCOLS], BF16)) for i in range(NBUF)]
        combb = [Ecm(nc.sbuf_tensor(f"combb{i}", [128, TC], BF16)) for i in range(2)]
        sqb = [Ecm(nc.sbuf_tensor(f"sqb{i}", [128, TC], BF16)) for i in range(2)]
        etmp = [Ecm(nc.sbuf_tensor("etmp0", [128, TC], F32))]
        mtmp = Ecm(nc.sbuf_tensor("mtmp", [128, TC], BF16))
        nm_s = Ecm(nc.sbuf_tensor("nm_s", [128, TC], BF16))
        sm = Ecm(nc.sbuf_tensor("sm", [1, 4 * TC], F32))
        lg = Ecm(nc.sbuf_tensor("lg", [8, TC], F32))
        combT = Ecm(nc.sbuf_tensor("combT", [8, TC], BF16))
        gsc = Ecm(nc.sbuf_tensor("gsc", [128, 320], F32))
        gateb = Ecm(nc.sbuf_tensor("gateb", [128, 128], BF16))
        dsb = Ecm(nc.sbuf_tensor("dsb", [128, 32], F32))
        idsb = Ecm(nc.sbuf_tensor("idsb", [128, 128], F32))
        onescol2 = Ecm(nc.sbuf_tensor("onescol_s", [128, 2], F32))
        onescol = onescol2[:, 0:1]
        epsap = onescol2[:, 1:2]
        onescolb = Ecm(nc.sbuf_tensor("onescolb", [128, 1], BF16))
        onesrow = Ecm(nc.sbuf_tensor("onesrow_s", [1, 128], F32))
        selb = Ecm(nc.sbuf_tensor("selb", [8, 1024], BF16))
        pb = [Ecm(nc.psum_tensor(f"pb{i}", [128, TC], F32)) for i in range(8)]

        WD = Ecm(nc.semaphore("WD"))
        XD = Ecm(nc.semaphore("XD"))
        OD = Ecm(nc.semaphore("OD"))
        CDg = Ecm(nc.semaphore("CDg"))
        CDs = Ecm(nc.semaphore("CDs"))
        MM = Ecm(nc.semaphore("MM"))
        SC = Ecm(nc.semaphore("SC"))
        VE = Ecm(nc.semaphore("VE"))
        block = Ecm(nc.Block())

        sched = defaultdict(list)

        def em(eng, fn):
            sched[eng].append(fn)

        cnt = {"MM": 0, "SC": 0, "VE": 0}
        SEM = {"MM": MM, "SC": SC, "VE": VE}
        bank_free = {i: ("VE", 0) for i in range(8)}

        def wait(eng, semname, val):
            if val > 0:
                s = {"WD": WD, "XD": XD, "OD": OD, "CDg": CDg,
                     "CDs": CDs, "MM": MM, "SC": SC, "VE": VE}[semname]
                em(eng, lambda e, s=s, v=val: e.wait_ge(s, v))

        def t_bankwait(bank):
            sn, v = bank_free[bank]
            wait("tensor", sn, v)

        def mm_em(out_ap, lhsT_ap, rhs_ap, start, stop, incs=(), skip_gc=False):
            def fn(t, o=out_ap, l=lhsT_ap, r=rhs_ap, s0=start, s1=stop,
                   incs=tuple(incs), sgc=skip_gc):
                ins = t.matmul(o, l, r, start=s0, stop=s1, skip_group_check=sgc)
                for n, (sem, k) in enumerate(incs):
                    if n == 0:
                        ins.then_inc(sem, k)
                    else:
                        t.nop().then_inc(sem, k)
            em("tensor", fn)

        def milestone(incs, name):
            cnt[name] += 1
            incs.append((SEM[name], 1))
            return cnt[name]

        items = []
        pan_mm_vals = {}
        pan = {"i": 0}

        def use_panel():
            g = pan["i"]
            pan["i"] += 1
            wait("tensor", "WD", 16 * (g + 1))
            return wb[g % NBUF], g

        def panel_done(g, mm_val):
            pan_mm_vals[g] = mm_val

        def ln_tile(k):
            return lnb[:, k * TC:(k + 1) * TC]

        def ub_tile(k):
            return ub[:, k * TC:(k + 1) * TC]

        def x0_tile(k):
            return x0[:, k * TC:(k + 1) * TC]

        s_mu = sm[0:1, 0 * TC:1 * TC]
        s_e2 = sm[0:1, 1 * TC:2 * TC]
        s_m2 = sm[0:1, 2 * TC:3 * TC]
        s_r = sm[0:1, 3 * TC:4 * TC]
        s_nmr = sm[0:1, 2 * TC:3 * TC]

        for eng in ("tensor", "scalar", "vector"):
            wait(eng, "CDg", 16 * 3)
            wait(eng, "CDs", 16 * 4)

        sq_war = [0, 0]
        cp_war = [0, 0]
        etmp_war = [0]
        comb_war = [0, 0]
        x2_done_val = []

        def ln_phase(c, which, xwait):
            t_bankwait(6)
            t_bankwait(7)
            sum_val = None
            cp_vals = {}
            for k in range(HT):
                sn, v = xwait[k]
                wait("vector", sn, v)
                vi = []
                cpv = milestone(vi, "VE")
                cp_vals[k] = cpv
                em("vector", lambda ve, k=k, incs=tuple(vi):
                   ve.tensor_copy(ln_tile(k), x0_tile(k)).then_inc(*incs[0]))
                par = k % 2
                wait("scalar", sn, v)
                if sq_war[par] > 0:
                    wait("scalar", "MM", sq_war[par])
                sc_inc = []
                sqv = milestone(sc_inc, "SC")
                em("scalar", lambda s, o=sqb[par][:, :], i=x0_tile(k), incs=tuple(sc_inc):
                   s.activation(o, i, AF.Square).then_inc(*incs[0]))
                wait("tensor", "VE", cpv)
                incs = []
                if k == HT - 1:
                    sum_val = milestone(incs, "MM")
                mm_em(pb[6][0:1, :], onescolb[:, :], ln_tile(k),
                      start=(k == 0), stop=(k == HT - 1), incs=incs, skip_gc=True)
                wait("tensor", "SC", sqv)
                incs = []
                ssq_val = milestone(incs, "MM")
                sq_war[par] = ssq_val
                mm_em(pb[7][0:1, :], onescolb[:, :], sqb[par][:, :],
                      start=(k == 0), stop=(k == HT - 1), incs=incs, skip_gc=True)
            sumsq_val = sq_war[(HT - 1) % 2]
            wait("vector", "MM", sumsq_val)
            vi = []
            milestone(vi, "VE")
            em("vector", lambda v, incs=tuple(vi):
               v.tensor_single_scalar(s_mu, pb[6][0:1, :], INV_H, ALU.mult).then_inc(*incs[0]))
            vi = []
            e2_val = milestone(vi, "VE")
            em("vector", lambda v, incs=tuple(vi):
               v.tensor_single_scalar(s_e2, pb[7][0:1, :], INV_H, ALU.mult).then_inc(*incs[0]))
            wait("scalar", "VE", e2_val)
            si = []
            mu2_val = milestone(si, "SC")
            em("scalar", lambda s, incs=tuple(si):
               s.activation(s_m2, s_mu, AF.Square).then_inc(*incs[0]))
            wait("vector", "SC", mu2_val)
            vi = []
            var_val = milestone(vi, "VE")
            em("vector", lambda v, incs=tuple(vi):
               v.tensor_tensor(s_e2, s_e2, s_m2, ALU.subtract).then_inc(*incs[0]))
            wait("scalar", "VE", var_val)
            em("scalar", lambda s: s.activation(s_m2, s_e2, AF.Ln, bias=epsap[0:1, :]))
            si = []
            r_val = milestone(si, "SC")
            em("scalar", lambda s, incs=tuple(si):
               s.activation(s_r, s_m2, AF.Exp, scale=-0.5).then_inc(*incs[0]))
            wait("vector", "SC", r_val)
            vi = []
            nmr_val = milestone(vi, "VE")
            em("vector", lambda v, incs=tuple(vi):
               v.scalar_tensor_tensor(s_nmr, s_mu, -1.0, s_r, ALU.mult, ALU.mult).then_inc(*incs[0]))
            wait("tensor", "VE", nmr_val)
            mm_em(pb[6][:, :], onesrow[:, :], s_r, start=True, stop=True, skip_gc=True)
            incs = []
            bc_val = milestone(incs, "MM")
            mm_em(pb[7][:, :], onesrow[:, :], s_nmr, start=True, stop=True,
                  incs=incs, skip_gc=True)
            wait("scalar", "MM", bc_val)
            si = []
            bc_ev = milestone(si, "SC")
            em("scalar", lambda s, incs=tuple(si):
               s.copy(nm_s[:, :], pb[7][:, :]).then_inc(*incs[0]))
            bank_free[7] = ("SC", bc_ev)
            wait("vector", "SC", bc_ev)
            wait("vector", "MM", bc_val)
            mat_vals = []
            for k in range(HT):
                em("vector", lambda v, k=k:
                   v.tensor_tensor(mtmp[:, :], x0_tile(k), pb[6][:, :], ALU.mult))
                vi = []
                mv = milestone(vi, "VE")
                mat_vals.append(mv)
                em("vector", lambda v, k=k, incs=tuple(vi):
                   v.tensor_tensor(ln_tile(k), mtmp[:, :], nm_s[:, :], ALU.add).then_inc(*incs[0]))
            ln_done = mat_vals[-1]
            bank_free[6] = ("VE", ln_done)
            return ln_done, mat_vals

        def up_phase(n_out, kt, rhs_fn, evac_fn, banks, rhs_ready=None,
                     kwaits=None):
            o0 = 0
            if kwaits is not None:
                assert kt == KB
                nb = len(banks)
                pans = []
                for o in range(nb):
                    t_bankwait(banks[o])
                    pans.append(use_panel())
                gvs = [None] * nb
                for k in range(kt):
                    sn, v = kwaits[k]
                    wait("tensor", sn, v)
                    for o in range(nb):
                        incs = []
                        if k == kt - 1:
                            gvs[o] = milestone(incs, "MM")
                        mm_em(pb[banks[o]][:, :], pans[o][0][:, k * 128:(k + 1) * 128],
                              rhs_fn(k), start=(k == 0), stop=(k == kt - 1),
                              incs=incs, skip_gc=True)
                for o in range(nb):
                    panel_done(pans[o][1], gvs[o])
                    bank_free[banks[o]] = evac_fn(o, banks[o], gvs[o])
                o0 = nb
            first = True
            nkb = (kt + KB - 1) // KB
            for o in range(o0, n_out):
                bank = banks[o % len(banks)]
                t_bankwait(bank)
                if first and rhs_ready is not None:
                    sn, v = rhs_ready
                    wait("tensor", sn, v)
                    first = False
                for kb in range(nkb):
                    buf, g = use_panel()
                    klen = min(KB, kt - kb * KB)
                    pmv = None
                    for kk in range(klen):
                        k = kb * KB + kk
                        incs = []
                        if kk == klen - 1:
                            pmv = milestone(incs, "MM")
                            if k == kt - 1:
                                gv = pmv
                        mm_em(pb[bank][:, :], buf[:, kk * 128:(kk + 1) * 128],
                              rhs_fn(k), start=(k == 0), stop=(k == kt - 1), incs=incs)
                    panel_done(g, pmv)
                bank_free[bank] = evac_fn(o, bank, gv)

        x1_done_val = [0]
        dbg_dump_vals = []
        OGROUPS = [(0, 4), (4, 4), (8, 4), (12, 1), (13, 1), (14, 1), (15, 1)]
        NOG = len(OGROUPS)
        xd_tile = []
        for c in range(NCH):
            per_tile = [0] * HT
            for gi, (st, ln) in enumerate(OGROUPS):
                for t in range(st, st + ln):
                    per_tile[t] = 16 * NOG * c + 16 * (gi + 1)
            xd_tile.append(per_tile)
        final_vals = []

        for c in range(NCH):
            ln1_done, mat1_vals = ln_phase(c, 1, [("XD", xd_tile[c][k]) for k in range(HT)])

            if STAGE >= 1:
                xp_done_val = [0]
                zs_vals = {}

                def win_evac(o, bank, gv):
                    wait("scalar", "MM", gv)
                    si = []
                    v = milestone(si, "SC")
                    if o < IT:
                        em("scalar", lambda s, o=o, bank=bank, incs=tuple(si):
                           s.copy(ub_tile(o), pb[bank][:, :]).then_inc(*incs[0]))
                        if o == IT - 1:
                            xp_done_val[0] = v
                    else:
                        em("scalar", lambda s, o=o, bank=bank, incs=tuple(si):
                           s.activation(ub_tile(o), pb[bank][:, :], AF.Silu).then_inc(*incs[0]))
                        zs_vals[o - IT] = v
                    return ("SC", v)

                for it in range(64):
                    items.append((wh["win"][it, :, :], PCOLS))
                up_phase(64, HT, ln_tile, win_evac, banks=(0, 1, 2, 3),
                         kwaits=[("VE", v) for v in mat1_vals])

                def conv_evac(o, bank, gv):
                    wait("vector", "MM", gv)
                    wait("vector", "SC", zs_vals[o])
                    vi = []
                    v = milestone(vi, "VE")
                    em("vector", lambda ve, o=o, bank=bank, incs=tuple(vi):
                       ve.scalar_tensor_tensor(ub_tile(64 + o), pb[bank][:, :],
                                               dsb[:, o:o + 1], ub_tile(IT + o),
                                               ALU.mult, ALU.mult).then_inc(*incs[0]))
                    return ("VE", v)

                for it in range(64):
                    items.append((wh["conv"][it, :, :], PCOLS))
                yz_start = cnt["VE"]
                up_phase(32, IT, lambda k: ub_tile(k), conv_evac, banks=(0, 1, 2, 3),
                         rhs_ready=("SC", xp_done_val[0]))
                yz_done = cnt["VE"]

                x1_tile_vals = {}

                def wout_evac(o, bank, gv):
                    wait("vector", "MM", gv)
                    vi = []
                    v = milestone(vi, "VE")
                    x1_tile_vals[o] = v
                    em("vector", lambda ve, o=o, bank=bank, incs=tuple(vi):
                       ve.tensor_tensor(x0_tile(o), x0_tile(o), pb[bank][:, :], ALU.add).then_inc(*incs[0]))
                    return ("VE", v)

                for it in range(32):
                    items.append((wh["wout"][it, :, :], PCOLS))
                up_phase(16, IT, lambda k: ub_tile(64 + k), wout_evac, banks=(4, 5),
                         rhs_ready=("VE", yz_done))
                x1_done_val[0] = cnt["VE"]
                dbg_dump_vals.append((c, cnt["VE"], cnt["SC"]))
            else:
                x1_done_val[0] = ln1_done

            if STAGE >= 2:
                ln2_done, mat2_vals = ln_phase(c, 2, [("VE", x1_tile_vals[k]) for k in range(HT)])

                for s0 in range(ST):
                    items.append((wh["sw1"][s0, :, :], PCOLS))
                    items.append((wh["sw2"][s0, :, :], PCOLS))

                t_bankwait(6)
                t_bankwait(7)
                ip = []
                for b in (0, 2, 1, 3):
                    t_bankwait(b)
                    ip.append(use_panel())
                igr = [(6, None), (0, ip[0][0]), (2, ip[1][0]), (1, ip[2][0]), (3, ip[3][0])]
                gvs = [None] * 5
                for k in range(HT):
                    wait("tensor", "VE", mat2_vals[k])
                    for gi, (bank, buf) in enumerate(igr):
                        incs = []
                        if k == HT - 1:
                            gvs[gi] = milestone(incs, "MM")
                        lhsT = gateb[:, k * 8:(k + 1) * 8] if buf is None \
                            else buf[:, k * 128:(k + 1) * 128]
                        out = pb[6][0:8, :] if buf is None else pb[bank][:, :]
                        mm_em(out, lhsT, ln_tile(k), start=(k == 0),
                              stop=(k == HT - 1), incs=incs, skip_gc=True)
                gate_val = gvs[0]
                for pi, g in enumerate((ip[0][1], ip[1][1], ip[2][1], ip[3][1])):
                    panel_done(g, gvs[1 + pi])
                wait("scalar", "MM", gate_val)
                si = []
                lg_val = milestone(si, "SC")
                em("scalar", lambda s, incs=tuple(si):
                   s.copy(lg[:, :], pb[6][0:8, :]).then_inc(*incs[0]))
                bank_free[6] = ("SC", lg_val)

                def sh_evac(s0, bank1, bank2, h1v, h2v):
                    wait("scalar", "MM", h1v)
                    if etmp_war[0] > 0:
                        wait("scalar", "VE", etmp_war[0])
                    si = []
                    sv = milestone(si, "SC")
                    em("scalar", lambda s, b1=bank1, incs=tuple(si):
                       s.activation(etmp[0][:, :], pb[b1][:, :], AF.Silu).then_inc(*incs[0]))
                    wait("vector", "MM", h2v)
                    wait("vector", "SC", sv)
                    vi = []
                    vv = milestone(vi, "VE")
                    em("vector", lambda v, b2=bank2, s0=s0, incs=tuple(vi):
                       v.tensor_tensor(ub_tile(FT + s0), etmp[0][:, :],
                                       pb[b2][:, :], ALU.mult).then_inc(*incs[0]))
                    etmp_war[0] = vv
                    bank_free[bank1] = ("SC", sv)
                    bank_free[bank2] = ("VE", vv)

                def sh_tile(s0):
                    bank1 = s0 % 2
                    bank2 = 2 + s0 % 2
                    buf1, g1 = use_panel()
                    t_bankwait(bank1)
                    for k in range(HT):
                        incs = []
                        if k == HT - 1:
                            h1v = milestone(incs, "MM")
                        mm_em(pb[bank1][:, :], buf1[:, k * 128:(k + 1) * 128],
                              ln_tile(k), start=(k == 0), stop=(k == HT - 1), incs=incs)
                    panel_done(g1, h1v)
                    buf2, g2 = use_panel()
                    t_bankwait(bank2)
                    for k in range(HT):
                        incs = []
                        if k == HT - 1:
                            h2v = milestone(incs, "MM")
                        mm_em(pb[bank2][:, :], buf2[:, k * 128:(k + 1) * 128],
                              ln_tile(k), start=(k == 0), stop=(k == HT - 1), incs=incs)
                    panel_done(g2, h2v)
                    sh_evac(s0, bank1, bank2, h1v, h2v)

                sh_evac(0, 0, 2, gvs[1], gvs[2])
                sh_evac(1, 1, 3, gvs[3], gvs[4])
                sh_tile(2)
                wait("tensor", "SC", lg_val)
                for g in range(4):
                    incs = []
                    if g == 3:
                        tp_val = milestone(incs, "MM")
                    def tp_fn(t, g=g, incs=tuple(incs)):
                        ins = t.transpose(pb[7][:, g * 8:(g + 1) * 8],
                                          lg[0:8, g * 128:(g + 1) * 128], idsb[0:8, 0:8])
                        for sem, k2 in incs:
                            ins.then_inc(sem, k2)
                    em("tensor", tp_fn)
                wait("vector", "MM", tp_val)
                vi = []
                tokcp_val = milestone(vi, "VE")
                em("vector", lambda v, incs=tuple(vi):
                   v.tensor_copy(gsc[:, 0:32], pb[7][:, 0:32]).then_inc(*incs[0]))
                wait("vector", "VE", tokcp_val)
                tok = lambda g: gsc[:, g * 8:(g + 1) * 8]
                e_g = lambda g: gsc[:, 32 + g * 8:32 + (g + 1) * 8]
                p_g = lambda g: gsc[:, 64 + g * 8:64 + (g + 1) * 8]
                scr = lambda g: gsc[:, 96 + g * 8:96 + (g + 1) * 8]
                cmb = lambda g: gsc[:, 128 + g * 8:128 + (g + 1) * 8]
                m1n4 = gsc[:, 160:164]
                ssum4 = gsc[:, 192:196]
                lns4 = gsc[:, 200:204]
                rs4 = gsc[:, 204:208]
                m24 = gsc[:, 224:228]
                m1p4 = gsc[:, 256:260]
                m1n = lambda g: gsc[:, 160 + g:161 + g]
                ssum = lambda g: gsc[:, 192 + g:193 + g]
                m2 = lambda g: gsc[:, 224 + g:225 + g]
                m1p = lambda g: gsc[:, 256 + g:257 + g]
                rs = lambda g: gsc[:, 204 + g:205 + g]
                grp3d = lambda base: gsc[:, base:base + 32].rearrange("p (g k) -> p g k", k=8)
                vi = []
                mx_val = milestone(vi, "VE")
                em("vector", lambda v, incs=tuple(vi):
                   v.reduce_max(m1n4, grp3d(0), AX.X).then_inc(*incs[0]))
                wait("vector", "VE", mx_val)
                vi = []
                m1_val = milestone(vi, "VE")
                em("vector", lambda v, incs=tuple(vi):
                   v.tensor_single_scalar(m1n4, m1n4, -1.0, ALU.mult).then_inc(*incs[0]))
                wait("scalar", "VE", m1_val)
                si = []
                exp_val = milestone(si, "SC")
                for g in range(4):
                    incs2 = tuple(si) if g == 3 else ()
                    def exp_fn(s, g=g, incs=incs2):
                        ins = s.activation(e_g(g), tok(g), AF.Exp, bias=m1n(g))
                        for sem, k2 in incs:
                            ins.then_inc(sem, k2)
                    em("scalar", exp_fn)
                wait("vector", "SC", exp_val)
                vi = []
                ssum_val = milestone(vi, "VE")
                em("vector", lambda v, incs=tuple(vi):
                   v.reduce_sum(ssum4, grp3d(32), AX.X).then_inc(*incs[0]))
                wait("scalar", "VE", ssum_val)
                em("scalar", lambda s: s.activation(lns4, ssum4, AF.Ln))
                si = []
                rs_val = milestone(si, "SC")
                em("scalar", lambda s, incs=tuple(si):
                   s.activation(rs4, lns4, AF.Exp, scale=-1.0).then_inc(*incs[0]))
                wait("vector", "SC", rs_val)
                for g in range(4):
                    if g == 3:
                        vi = []
                        pdone_val = milestone(vi, "VE")
                        em("vector", lambda v, g=g, incs=tuple(vi):
                           v.tensor_scalar(p_g(g), e_g(g), rs(g), None, ALU.mult).then_inc(*incs[0]))
                    else:
                        em("vector", lambda v, g=g:
                           v.tensor_scalar(p_g(g), e_g(g), rs(g), None, ALU.mult))
                wait("vector", "VE", pdone_val)
                vi = []
                m1p_val = milestone(vi, "VE")
                em("vector", lambda v, incs=tuple(vi):
                   v.reduce_max(m1p4, grp3d(64), AX.X).then_inc(*incs[0]))
                wait("vector", "VE", m1p_val)
                for g in range(4):
                    if g == 3:
                        vi = []
                        ge1_val = milestone(vi, "VE")
                        em("vector", lambda v, g=g, incs=tuple(vi):
                           v.tensor_scalar(scr(g), p_g(g), m1p(g), None, ALU.is_ge).then_inc(*incs[0]))
                    else:
                        em("vector", lambda v, g=g:
                           v.tensor_scalar(scr(g), p_g(g), m1p(g), None, ALU.is_ge))
                wait("vector", "VE", ge1_val)
                for g in range(4):
                    if g == 3:
                        vi = []
                        pm_val = milestone(vi, "VE")
                        em("vector", lambda v, g=g, incs=tuple(vi):
                           v.tensor_tensor(e_g(g), p_g(g), scr(g), ALU.subtract).then_inc(*incs[0]))
                    else:
                        em("vector", lambda v, g=g:
                           v.tensor_tensor(e_g(g), p_g(g), scr(g), ALU.subtract))
                wait("vector", "VE", pm_val)
                vi = []
                m2_val = milestone(vi, "VE")
                em("vector", lambda v, incs=tuple(vi):
                   v.reduce_max(m24, grp3d(32), AX.X).then_inc(*incs[0]))
                wait("vector", "VE", m2_val)
                for g in range(4):
                    if g == 3:
                        vi = []
                        ge2_val = milestone(vi, "VE")
                        em("vector", lambda v, g=g, incs=tuple(vi):
                           v.tensor_scalar(scr(g), p_g(g), m2(g), None, ALU.is_ge).then_inc(*incs[0]))
                    else:
                        em("vector", lambda v, g=g:
                           v.tensor_scalar(scr(g), p_g(g), m2(g), None, ALU.is_ge))
                wait("vector", "VE", ge2_val)
                for g in range(4):
                    if g == 3:
                        vi = []
                        comb_val = milestone(vi, "VE")
                        em("vector", lambda v, g=g, incs=tuple(vi):
                           v.tensor_tensor(cmb(g), p_g(g), scr(g), ALU.mult).then_inc(*incs[0]))
                    else:
                        em("vector", lambda v, g=g:
                           v.tensor_tensor(cmb(g), p_g(g), scr(g), ALU.mult))
                sh_tile(3)
                sh_tile(4)
                wait("tensor", "VE", comb_val)
                for g in range(4):
                    incs = []
                    if g == 3:
                        btp_val = milestone(incs, "MM")
                    def btp_fn(t, g=g, incs=tuple(incs)):
                        ins = t.transpose(pb[6][0:8, g * 128:(g + 1) * 128],
                                          cmb(g), idsb[:, 0:128])
                        for sem, k2 in incs:
                            ins.then_inc(sem, k2)
                    em("tensor", btp_fn)
                wait("scalar", "MM", btp_val)
                si = []
                combT_val = milestone(si, "SC")
                em("scalar", lambda s, incs=tuple(si):
                   s.copy(combT[:, :], pb[6][0:8, :]).then_inc(*incs[0]))
                bank_free[6] = ("SC", combT_val)
                bank_free[7] = ("VE", comb_val)
                for s0 in range(5, ST):
                    sh_tile(s0)

                for f in range(FT):
                    items.append((wh["w1"][f, :, :], PCOLS))
                    items.append((wh["w2"][f, :, :], PCOLS))

                combb_vals = {}

                def jit_selector(e):
                    bank = 6 if e % 2 == 0 else 7
                    t_bankwait(bank)
                    wait("tensor", "SC", combT_val)
                    incs = []
                    sv = milestone(incs, "MM")
                    mm_em(pb[bank][0:128, :], selb[0:8, e * 128:(e + 1) * 128], combT[:, :],
                          start=True, stop=True, incs=incs, skip_gc=True)
                    wait("scalar", "MM", sv)
                    if comb_war[e % 2] > 0:
                        wait("scalar", "VE", comb_war[e % 2])
                    si = []
                    cv = milestone(si, "SC")
                    em("scalar", lambda s, e=e, bank=bank, incs=tuple(si):
                       s.copy(combb[e % 2][:, :], pb[bank][:, :]).then_inc(*incs[0]))
                    combb_vals[e] = cv
                    bank_free[bank] = ("SC", cv)

                for f in range(FT):
                    e = f // IET
                    if f % IET == 0:
                        jit_selector(e)
                    bank1 = f % 2
                    bank2 = 2 + f % 2
                    buf1, g1 = use_panel()
                    t_bankwait(bank1)
                    for k in range(HT):
                        incs = []
                        if k == HT - 1:
                            h1v = milestone(incs, "MM")
                        mm_em(pb[bank1][:, :], buf1[:, k * 128:(k + 1) * 128],
                              ln_tile(k), start=(k == 0), stop=(k == HT - 1), incs=incs)
                    panel_done(g1, h1v)
                    buf2, g2 = use_panel()
                    t_bankwait(bank2)
                    for k in range(HT):
                        incs = []
                        if k == HT - 1:
                            h2v = milestone(incs, "MM")
                        mm_em(pb[bank2][:, :], buf2[:, k * 128:(k + 1) * 128],
                              ln_tile(k), start=(k == 0), stop=(k == HT - 1), incs=incs)
                    panel_done(g2, h2v)
                    wait("scalar", "MM", h1v)
                    if etmp_war[0] > 0:
                        wait("scalar", "VE", etmp_war[0])
                    si = []
                    sv = milestone(si, "SC")
                    em("scalar", lambda s, b1=bank1, incs=tuple(si):
                       s.activation(etmp[0][:, :], pb[b1][:, :], AF.Silu).then_inc(*incs[0]))
                    wait("vector", "MM", h2v)
                    wait("vector", "SC", sv)
                    wait("vector", "SC", combb_vals[e])
                    em("vector", lambda v, b2=bank2:
                       v.tensor_tensor(etmp[0][:, :], etmp[0][:, :], pb[b2][:, :], ALU.mult))
                    vi = []
                    vv = milestone(vi, "VE")
                    em("vector", lambda v, f=f, e=e, incs=tuple(vi):
                       v.tensor_tensor(ub_tile(f), etmp[0][:, :], combb[e % 2][:, :],
                                       ALU.mult).then_inc(*incs[0]))
                    etmp_war[0] = vv
                    comb_war[e % 2] = vv
                    bank_free[bank1] = ("SC", sv)
                    bank_free[bank2] = ("VE", vv)
                hprime_done = cnt["VE"]

                for ct in range(16):
                    for b3 in range(6):
                        items.append((wh["d16"][ct * 6 + b3, :, :], PCOLS))
                    items.append((wh["d14"][ct, :, :], 14 * 128))

                nblk = len(DOWN_BLOCKS)
                x2_tile_vals = []
                for ct in range(16):
                    bank = 4 + (ct % 2)
                    t_bankwait(bank)
                    if ct == 0:
                        wait("tensor", "VE", hprime_done)
                    nmm = 0
                    for bi, (kst, klen) in enumerate(DOWN_BLOCKS):
                        buf, g = use_panel()
                        for kk in range(klen):
                            gk = kst + kk
                            nmm += 1
                            incs = []
                            if kk == klen - 1:
                                pv = milestone(incs, "MM")
                                if bi == nblk - 1:
                                    gv = pv
                            mm_em(pb[bank][:, :], buf[:, kk * 128:(kk + 1) * 128],
                                  ub_tile(gk), start=(nmm == 1),
                                  stop=(bi == nblk - 1 and kk == klen - 1),
                                  incs=incs)
                        panel_done(g, pv)
                    wait("vector", "MM", gv)
                    vi = []
                    vv = milestone(vi, "VE")
                    x2_tile_vals.append(vv)
                    em("vector", lambda v, ct=ct, bank=bank, incs=tuple(vi):
                       v.tensor_tensor(x0_tile(ct), x0_tile(ct), pb[bank][:, :],
                                       ALU.add).then_inc(*incs[0]))
                    bank_free[bank] = ("VE", vv)
                x2_done_val.append(cnt["VE"])
                final_vals.append(x2_tile_vals)
                if STAGE >= 2:
                    dbg_dump_vals[c] = (c, cnt["VE"], cnt["SC"])
            else:
                x2_done_val.append(x1_done_val[0])
                if STAGE == 1:
                    final_vals.append([x1_tile_vals[o] for o in range(16)])
                else:
                    final_vals.append([ln1_done] * 16)

        n_items = len(items)

        @block.gpsimd
        def _(g):
            g.dma_start(out=gateb[:, :], in_=gate_h[:, :]).then_inc(CDg, 16)
            g.dma_start(out=selb[:, :], in_=sel_h[:, :]).then_inc(CDg, 16)
            g.dma_start(out=onescolb[:, :], in_=ones1_h[:, :]).then_inc(CDg, 16)
            for gidx, (ap, cols) in enumerate(items):
                if gidx >= NBUF:
                    g.wait_ge(MM, pan_mm_vals[gidx - NBUF])
                g.dma_start(out=wb[gidx % NBUF][:, 0:cols], in_=ap).then_inc(WD, 16)

        @block.sync
        def _(sy):
            sy.dma_start(out=dsb[:, :], in_=dvec_h[:, :]).then_inc(CDs, 16)
            sy.dma_start(out=idsb[:, :], in_=ident_h[:, :]).then_inc(CDs, 16)
            sy.dma_start(out=onescol2[:, :], in_=onescol_h[:, :]).then_inc(CDs, 16)
            sy.dma_start(out=onesrow[:, :], in_=onesrow_h[:, :]).then_inc(CDs, 16)
            n_od = 0
            for (st, ln) in OGROUPS:
                sy.dma_start(out=x0[:, st * TC:(st + ln) * TC],
                             in_=x0t_h[0, :, st * TC:(st + ln) * TC]).then_inc(XD, 16)
            for c in range(NCH):
                if DEBUG_DUMP and c < len(dbg_dump_vals):
                    _, vev, scv = dbg_dump_vals[c]
                    sy.wait_ge(VE, vev)
                    sy.wait_ge(SC, scv)
                    sy.dma_start(out=dbgu_h[c, :, :], in_=ub[:, :]).then_inc(OD, 16)
                    sy.dma_start(out=dbgl_h[c, :, :], in_=lnb[:, :]).then_inc(OD, 16)
                    sy.dma_start(out=dbgc_h[c, :, :], in_=combT[:, :]).then_inc(OD, 16)
                    sy.dma_start(out=dbgg_h[c, :, :], in_=gsc[:, :]).then_inc(OD, 16)
                    sy.dma_start(out=dbgs_h[c, :, :], in_=lg[:, :]).then_inc(OD, 16)
                    n_od += 5
                for (st, ln) in OGROUPS:
                    sy.wait_ge(VE, final_vals[c][st + ln - 1])
                    sy.dma_start(out=out_h[c, :, st * TC:(st + ln) * TC],
                                 in_=x0[:, st * TC:(st + ln) * TC]).then_inc(OD, 16)
                    n_od += 1
                    if c + 1 < NCH:
                        sy.dma_start(out=x0[:, st * TC:(st + ln) * TC],
                                     in_=x0t_h[c + 1, :, st * TC:(st + ln) * TC]).then_inc(XD, 16)
            sy.wait_ge(OD, 16 * n_od)

        @block.tensor
        def _(t):
            for fn in sched["tensor"]:
                fn(t)

        @block.scalar
        def _(s):
            for fn in sched["scalar"]:
                fn(s)

        @block.vector
        def _(v):
            for fn in sched["vector"]:
                fn(v)

    return nc



def _up_panels(W):
    O, Kh = W.shape
    K = Kh // 128
    nkb = K // 16
    A = W.reshape(O // 128, 128, nkb, 16, 128)
    return np.ascontiguousarray(A.transpose(0, 2, 4, 3, 1)).reshape(-1, 128, 16 * 128)


def kernel(L0_attn_hidden_states, L0_attn_W_in, L0_attn_conv_w, L0_attn_W_x,
           L0_attn_W_dt, L0_attn_b_dt, L0_attn_D, L0_attn_W_out,
           L0_ffn_gate_w, L0_ffn_w1, L0_ffn_w2, L0_ffn_w3,
           L0_ffn_sw1, L0_ffn_sw2, L0_ffn_sw3):
    f32 = np.float32
    x = np.ascontiguousarray(np.asarray(L0_attn_hidden_states, dtype=f32).reshape(16384, 2048))
    W_in = np.asarray(L0_attn_W_in, f32)
    conv_w = np.asarray(L0_attn_conv_w, f32)
    D = np.asarray(L0_attn_D, f32)
    W_out = np.asarray(L0_attn_W_out, f32)
    gate_w = np.asarray(L0_ffn_gate_w, f32)
    w1 = np.asarray(L0_ffn_w1, f32)
    w2 = np.asarray(L0_ffn_w2, f32)
    w3 = np.asarray(L0_ffn_w3, f32)
    sw1 = np.asarray(L0_ffn_sw1, f32)
    sw2 = np.asarray(L0_ffn_sw2, f32)
    sw3 = np.asarray(L0_ffn_sw3, f32)

    weights = {
        "win": _up_panels(W_in),
        "conv": _up_panels(conv_w),
        "wout": _up_panels(W_out),
        "w1": _up_panels(w1.reshape(8 * 1408, 2048)),
        "w2": _up_panels(w2.reshape(8 * 1408, 2048)),
        "sw1": _up_panels(sw1),
        "sw2": _up_panels(sw2),
        "gate": np.ascontiguousarray(gate_w.reshape(8, 16, 128).transpose(2, 1, 0)).reshape(128, 128),
        "dvec": np.ascontiguousarray(D.reshape(32, 128).T),
        "ident": np.eye(128, dtype=f32),
        "ones1": np.ones((128, 1), f32),
        "onescol": np.concatenate([np.ones((128, 1), f32),
                                   np.full((128, 1), EPS, f32)], axis=1),
        "onesrow": np.ones((1, 128), f32),
    }
    Wd = np.concatenate([np.ascontiguousarray(w3.transpose(0, 2, 1)).reshape(8 * 1408, 2048),
                         np.ascontiguousarray(sw3.T)], axis=0)
    A = Wd.reshape(110, 128, 16, 128)
    d16 = np.empty((96, 128, PCOLS), f32)
    d14 = np.empty((16, 128, 14 * 128), f32)
    for ct in range(16):
        for b in range(6):
            d16[ct * 6 + b] = A[b * 16:(b + 1) * 16, :, ct, :].transpose(1, 0, 2).reshape(128, PCOLS)
        d14[ct] = A[96:110, :, ct, :].transpose(1, 0, 2).reshape(128, 14 * 128)
    weights["d16"] = np.ascontiguousarray(d16)
    weights["d14"] = np.ascontiguousarray(d14)
    sel = np.zeros((8, 1024), f32)
    for e in range(8):
        sel[e, e * 128:(e + 1) * 128] = 1.0
    weights["sel"] = sel

    nc = build_program()

    in_maps = []
    for core in range(NCORE):
        xc = x[core * TCORE:(core + 1) * TCORE, :]
        x0t = np.ascontiguousarray(
            xc.reshape(NCH, TC, HT, 128).transpose(0, 3, 2, 1)).reshape(NCH, 128, HT * TC)
        in_maps.append({**weights, "x0t": x0t})

    res = run_bass_kernel_spmd(nc, in_maps, core_ids=list(range(NCORE)))

    out = np.empty((16384, 2048), f32)
    for core in range(NCORE):
        o = np.asarray(res.results[core]["out"])
        oc = o.reshape(NCH, 128, HT, TC).transpose(0, 3, 2, 1).reshape(TCORE, 2048)
        out[core * TCORE:(core + 1) * TCORE, :] = oc
    return out.reshape(1, 16384, 2048)


if __name__ == "__main__":
    pass


# version 08
# speedup vs baseline: 1.2124x; 1.2124x over previous
import contextlib
from collections import defaultdict

import numpy as np
import concourse.bass as bass
import concourse.mybir as mybir
from concourse.bass_utils import run_bass_kernel_spmd

F32 = mybir.dt.float32
BF16 = mybir.dt.bfloat16
AF = mybir.ActivationFunctionType
ALU = mybir.AluOpType
AX = mybir.AxisListType

NCORE = 8
H = 2048
HT = 16
IT = 32
E = 8
IET = 11
FT = 88
ST = 22
DKT = FT + ST
TC = 512
NCH = 4
TCORE = TC * NCH
NBUF = 6
PCOLS = 2048
KB = 16
EPS = 1e-5
INV_H = 1.0 / H
DOWN_BLOCKS = [(16 * i, 16) for i in range(6)] + [(96, 14)]

STAGE = 2
DEBUG_DUMP = False


def build_program():
    nc = bass.Bass("TRN2", target_bir_lowering=False, debug=False)

    x0t_h = nc.dram_tensor("x0t", [NCH, 128, HT * TC], F32, kind="ExternalInput").ap()
    out_h = nc.dram_tensor("out", [NCH, 128, HT * TC], F32, kind="ExternalOutput").ap()
    wh = {}
    for name, npan, cols in [
        ("win", 64, PCOLS), ("conv", 64, PCOLS), ("wout", 32, PCOLS),
        ("w1", 88, PCOLS), ("w2", 88, PCOLS), ("sw1", 22, PCOLS), ("sw2", 22, PCOLS),
        ("d16", 96, PCOLS), ("d14", 16, 14 * 128),
    ]:
        wh[name] = nc.dram_tensor(name, [npan, 128, cols], F32, kind="ExternalInput").ap()
    gate_h = nc.dram_tensor("gate", [128, 128], F32, kind="ExternalInput").ap()
    dvec_h = nc.dram_tensor("dvec", [128, 32], F32, kind="ExternalInput").ap()
    ident_h = nc.dram_tensor("ident", [128, 128], F32, kind="ExternalInput").ap()
    onescol_h = nc.dram_tensor("onescol", [128, 2], F32, kind="ExternalInput").ap()
    onesrow_h = nc.dram_tensor("onesrow", [1, 128], F32, kind="ExternalInput").ap()
    sel_h = nc.dram_tensor("sel", [8, 1024], F32, kind="ExternalInput").ap()
    ones1_h = nc.dram_tensor("ones1", [128, 1], F32, kind="ExternalInput").ap()
    if DEBUG_DUMP:
        dbgu_h = nc.dram_tensor("dbgu", [NCH, 128, DKT * TC], BF16, kind="ExternalOutput").ap()
        dbgl_h = nc.dram_tensor("dbgl", [NCH, 128, HT * TC], BF16, kind="ExternalOutput").ap()
        dbgc_h = nc.dram_tensor("dbgc", [NCH, 8, TC], BF16, kind="ExternalOutput").ap()
        dbgg_h = nc.dram_tensor("dbgg", [NCH, 128, 320], F32, kind="ExternalOutput").ap()
        dbgs_h = nc.dram_tensor("dbgs", [NCH, 8, TC], F32, kind="ExternalOutput").ap()

    with contextlib.ExitStack() as stack:
        Ecm = stack.enter_context
        x0 = Ecm(nc.sbuf_tensor("x0", [128, HT * TC], F32))
        lnb = Ecm(nc.sbuf_tensor("lnb", [128, HT * TC], BF16))
        ub = Ecm(nc.sbuf_tensor("ub", [128, DKT * TC], BF16))
        wb = [Ecm(nc.sbuf_tensor(f"wb{i}", [128, PCOLS], BF16)) for i in range(NBUF)]
        combb = [Ecm(nc.sbuf_tensor(f"combb{i}", [128, TC], BF16)) for i in range(2)]
        sqb = [Ecm(nc.sbuf_tensor(f"sqb{i}", [128, TC], BF16)) for i in range(2)]
        etmp = [Ecm(nc.sbuf_tensor("etmp0", [128, TC], F32))]
        mtmp = Ecm(nc.sbuf_tensor("mtmp", [128, TC], BF16))
        nm_s = Ecm(nc.sbuf_tensor("nm_s", [128, TC], BF16))
        sm = Ecm(nc.sbuf_tensor("sm", [1, 4 * TC], F32))
        lg = Ecm(nc.sbuf_tensor("lg", [8, TC], F32))
        combT = Ecm(nc.sbuf_tensor("combT", [8, TC], BF16))
        gsc = Ecm(nc.sbuf_tensor("gsc", [128, 320], F32))
        gateb = Ecm(nc.sbuf_tensor("gateb", [128, 128], BF16))
        dsb = Ecm(nc.sbuf_tensor("dsb", [128, 32], F32))
        idsb = Ecm(nc.sbuf_tensor("idsb", [128, 128], F32))
        onescol2 = Ecm(nc.sbuf_tensor("onescol_s", [128, 2], F32))
        onescol = onescol2[:, 0:1]
        epsap = onescol2[:, 1:2]
        onescolb = Ecm(nc.sbuf_tensor("onescolb", [128, 1], BF16))
        onesrow = Ecm(nc.sbuf_tensor("onesrow_s", [1, 128], F32))
        selb = Ecm(nc.sbuf_tensor("selb", [8, 1024], BF16))
        pb = [Ecm(nc.psum_tensor(f"pb{i}", [128, TC], F32)) for i in range(8)]

        WD = Ecm(nc.semaphore("WD"))
        XD = Ecm(nc.semaphore("XD"))
        OD = Ecm(nc.semaphore("OD"))
        CDg = Ecm(nc.semaphore("CDg"))
        CDs = Ecm(nc.semaphore("CDs"))
        MM = Ecm(nc.semaphore("MM"))
        SC = Ecm(nc.semaphore("SC"))
        VE = Ecm(nc.semaphore("VE"))
        block = Ecm(nc.Block())

        sched = defaultdict(list)

        def em(eng, fn):
            sched[eng].append(fn)

        cnt = {"MM": 0, "SC": 0, "VE": 0}
        SEM = {"MM": MM, "SC": SC, "VE": VE}
        bank_free = {i: ("VE", 0) for i in range(8)}

        def wait(eng, semname, val):
            if val > 0:
                s = {"WD": WD, "XD": XD, "OD": OD, "CDg": CDg,
                     "CDs": CDs, "MM": MM, "SC": SC, "VE": VE}[semname]
                em(eng, lambda e, s=s, v=val: e.wait_ge(s, v))

        def t_bankwait(bank):
            sn, v = bank_free[bank]
            wait("tensor", sn, v)

        def mm_em(out_ap, lhsT_ap, rhs_ap, start, stop, incs=(), skip_gc=False):
            def fn(t, o=out_ap, l=lhsT_ap, r=rhs_ap, s0=start, s1=stop,
                   incs=tuple(incs), sgc=skip_gc):
                ins = t.matmul(o, l, r, start=s0, stop=s1, skip_group_check=sgc)
                for n, (sem, k) in enumerate(incs):
                    if n == 0:
                        ins.then_inc(sem, k)
                    else:
                        t.nop().then_inc(sem, k)
            em("tensor", fn)

        def milestone(incs, name):
            cnt[name] += 1
            incs.append((SEM[name], 1))
            return cnt[name]

        items = []
        pan_mm_vals = {}
        pan = {"i": 0}

        def use_panel():
            g = pan["i"]
            pan["i"] += 1
            wait("tensor", "WD", 16 * (g + 1))
            return wb[g % NBUF], g

        def panel_done(g, mm_val):
            pan_mm_vals[g] = mm_val

        def ln_tile(k):
            return lnb[:, k * TC:(k + 1) * TC]

        def ub_tile(k):
            return ub[:, k * TC:(k + 1) * TC]

        def x0_tile(k):
            return x0[:, k * TC:(k + 1) * TC]

        s_mu = sm[0:1, 0 * TC:1 * TC]
        s_e2 = sm[0:1, 1 * TC:2 * TC]
        s_m2 = sm[0:1, 2 * TC:3 * TC]
        s_r = sm[0:1, 3 * TC:4 * TC]
        s_nmr = sm[0:1, 2 * TC:3 * TC]

        for eng in ("tensor", "scalar", "vector"):
            wait(eng, "CDg", 16 * 3)
            wait(eng, "CDs", 16 * 4)

        sq_war = [0, 0]
        ub_vals = {}
        cp_war = [0, 0]
        etmp_war = [0]
        comb_war = [0, 0]
        x2_done_val = []

        def ln_phase(c, which, xwait):
            t_bankwait(6)
            t_bankwait(7)
            sum_val = None
            cp_vals = {}
            for k in range(HT):
                sn, v = xwait[k]
                wait("vector", sn, v)
                vi = []
                cpv = milestone(vi, "VE")
                cp_vals[k] = cpv
                em("vector", lambda ve, k=k, incs=tuple(vi):
                   ve.tensor_copy(ln_tile(k), x0_tile(k)).then_inc(*incs[0]))
                par = k % 2
                wait("scalar", sn, v)
                if sq_war[par] > 0:
                    wait("scalar", "MM", sq_war[par])
                sc_inc = []
                sqv = milestone(sc_inc, "SC")
                em("scalar", lambda s, o=sqb[par][:, :], i=x0_tile(k), incs=tuple(sc_inc):
                   s.activation(o, i, AF.Square).then_inc(*incs[0]))
                wait("tensor", "VE", cpv)
                incs = []
                if k == HT - 1:
                    sum_val = milestone(incs, "MM")
                mm_em(pb[6][0:1, :], onescolb[:, :], ln_tile(k),
                      start=(k == 0), stop=(k == HT - 1), incs=incs, skip_gc=True)
                wait("tensor", "SC", sqv)
                incs = []
                ssq_val = milestone(incs, "MM")
                sq_war[par] = ssq_val
                mm_em(pb[7][0:1, :], onescolb[:, :], sqb[par][:, :],
                      start=(k == 0), stop=(k == HT - 1), incs=incs, skip_gc=True)
            sumsq_val = sq_war[(HT - 1) % 2]
            wait("vector", "MM", sumsq_val)
            vi = []
            milestone(vi, "VE")
            em("vector", lambda v, incs=tuple(vi):
               v.tensor_single_scalar(s_mu, pb[6][0:1, :], INV_H, ALU.mult).then_inc(*incs[0]))
            vi = []
            e2_val = milestone(vi, "VE")
            em("vector", lambda v, incs=tuple(vi):
               v.tensor_single_scalar(s_e2, pb[7][0:1, :], INV_H, ALU.mult).then_inc(*incs[0]))
            wait("scalar", "VE", e2_val)
            si = []
            mu2_val = milestone(si, "SC")
            em("scalar", lambda s, incs=tuple(si):
               s.activation(s_m2, s_mu, AF.Square).then_inc(*incs[0]))
            wait("vector", "SC", mu2_val)
            vi = []
            var_val = milestone(vi, "VE")
            em("vector", lambda v, incs=tuple(vi):
               v.tensor_tensor(s_e2, s_e2, s_m2, ALU.subtract).then_inc(*incs[0]))
            wait("scalar", "VE", var_val)
            em("scalar", lambda s: s.activation(s_m2, s_e2, AF.Ln, bias=epsap[0:1, :]))
            si = []
            r_val = milestone(si, "SC")
            em("scalar", lambda s, incs=tuple(si):
               s.activation(s_r, s_m2, AF.Exp, scale=-0.5).then_inc(*incs[0]))
            wait("vector", "SC", r_val)
            vi = []
            nmr_val = milestone(vi, "VE")
            em("vector", lambda v, incs=tuple(vi):
               v.scalar_tensor_tensor(s_nmr, s_mu, -1.0, s_r, ALU.mult, ALU.mult).then_inc(*incs[0]))
            wait("tensor", "VE", nmr_val)
            mm_em(pb[6][:, :], onesrow[:, :], s_r, start=True, stop=True, skip_gc=True)
            incs = []
            bc_val = milestone(incs, "MM")
            mm_em(pb[7][:, :], onesrow[:, :], s_nmr, start=True, stop=True,
                  incs=incs, skip_gc=True)
            wait("scalar", "MM", bc_val)
            si = []
            bc_ev = milestone(si, "SC")
            em("scalar", lambda s, incs=tuple(si):
               s.copy(nm_s[:, :], pb[7][:, :]).then_inc(*incs[0]))
            bank_free[7] = ("SC", bc_ev)
            wait("vector", "MM", bc_val)
            mat_vals = []
            for k in range(HT):
                em("vector", lambda v, k=k:
                   v.tensor_tensor(mtmp[:, :], x0_tile(k), pb[6][:, :], ALU.mult))
                if k == 0:
                    wait("vector", "SC", bc_ev)
                vi = []
                mv = milestone(vi, "VE")
                mat_vals.append(mv)
                em("vector", lambda v, k=k, incs=tuple(vi):
                   v.tensor_tensor(ln_tile(k), mtmp[:, :], nm_s[:, :], ALU.add).then_inc(*incs[0]))
            ln_done = mat_vals[-1]
            bank_free[6] = ("VE", ln_done)
            return ln_done, mat_vals

        def up_phase(n_out, kt, rhs_fn, evac_fn, banks, rhs_ready=None,
                     kwaits=None):
            o0 = 0
            if kwaits is not None:
                assert kt == KB
                nb = len(banks)
                pans = []
                for o in range(nb):
                    t_bankwait(banks[o])
                    pans.append(use_panel())
                gvs = [None] * nb
                for k in range(kt):
                    sn, v = kwaits[k]
                    wait("tensor", sn, v)
                    for o in range(nb):
                        incs = []
                        if k == kt - 1:
                            gvs[o] = milestone(incs, "MM")
                        mm_em(pb[banks[o]][:, :], pans[o][0][:, k * 128:(k + 1) * 128],
                              rhs_fn(k), start=(k == 0), stop=(k == kt - 1),
                              incs=incs, skip_gc=True)
                for o in range(nb):
                    panel_done(pans[o][1], gvs[o])
                    bank_free[banks[o]] = evac_fn(o, banks[o], gvs[o])
                o0 = nb
            first = True
            nkb = (kt + KB - 1) // KB
            for o in range(o0, n_out):
                bank = banks[o % len(banks)]
                t_bankwait(bank)
                if first and rhs_ready is not None:
                    sn, v = rhs_ready
                    wait("tensor", sn, v)
                    first = False
                for kb in range(nkb):
                    buf, g = use_panel()
                    klen = min(KB, kt - kb * KB)
                    pmv = None
                    for kk in range(klen):
                        k = kb * KB + kk
                        incs = []
                        if kk == klen - 1:
                            pmv = milestone(incs, "MM")
                            if k == kt - 1:
                                gv = pmv
                        mm_em(pb[bank][:, :], buf[:, kk * 128:(kk + 1) * 128],
                              rhs_fn(k), start=(k == 0), stop=(k == kt - 1), incs=incs)
                    panel_done(g, pmv)
                bank_free[bank] = evac_fn(o, bank, gv)

        x1_done_val = [0]
        dbg_dump_vals = []
        OGROUPS = [(0, 4), (4, 4), (8, 4), (12, 1), (13, 1), (14, 1), (15, 1)]
        NOG = len(OGROUPS)
        xd_tile = []
        for c in range(NCH):
            per_tile = [0] * HT
            for gi, (st, ln) in enumerate(OGROUPS):
                for t in range(st, st + ln):
                    per_tile[t] = 16 * NOG * c + 16 * (gi + 1)
            xd_tile.append(per_tile)
        final_vals = []

        for c in range(NCH):
            ln1_done, mat1_vals = ln_phase(c, 1, [("XD", xd_tile[c][k]) for k in range(HT)])

            if STAGE >= 1:
                xp_done_val = [0]
                zs_vals = {}

                def win_evac(o, bank, gv):
                    wait("scalar", "MM", gv)
                    si = []
                    v = milestone(si, "SC")
                    if o < IT:
                        em("scalar", lambda s, o=o, bank=bank, incs=tuple(si):
                           s.copy(ub_tile(o), pb[bank][:, :]).then_inc(*incs[0]))
                        if o == IT - 1:
                            xp_done_val[0] = v
                    else:
                        em("scalar", lambda s, o=o, bank=bank, incs=tuple(si):
                           s.activation(ub_tile(o), pb[bank][:, :], AF.Silu).then_inc(*incs[0]))
                        zs_vals[o - IT] = v
                        if o == 63:
                            em("scalar", lambda s:
                               s.activation(gsc[0:1, 208:209], epsap[0:1, :], AF.Ln))
                    return ("SC", v)

                for it in range(64):
                    items.append((wh["win"][it, :, :], PCOLS))
                up_phase(64, HT, ln_tile, win_evac, banks=(0, 1, 2, 3),
                         kwaits=[("VE", v) for v in mat1_vals])

                def conv_evac(o, bank, gv):
                    wait("vector", "MM", gv)
                    wait("vector", "SC", zs_vals[o])
                    vi = []
                    v = milestone(vi, "VE")
                    em("vector", lambda ve, o=o, bank=bank, incs=tuple(vi):
                       ve.scalar_tensor_tensor(ub_tile(64 + o), pb[bank][:, :],
                                               dsb[:, o:o + 1], ub_tile(IT + o),
                                               ALU.mult, ALU.mult).then_inc(*incs[0]))
                    return ("VE", v)

                for it in range(64):
                    items.append((wh["conv"][it, :, :], PCOLS))
                yz_start = cnt["VE"]
                up_phase(32, IT, lambda k: ub_tile(k), conv_evac, banks=(0, 1, 2, 3),
                         rhs_ready=("SC", xp_done_val[0]))
                yz_done = cnt["VE"]

                x1_tile_vals = {}

                def wout_evac(o, bank, gv):
                    wait("vector", "MM", gv)
                    vi = []
                    v = milestone(vi, "VE")
                    x1_tile_vals[o] = v
                    em("vector", lambda ve, o=o, bank=bank, incs=tuple(vi):
                       ve.tensor_tensor(x0_tile(o), x0_tile(o), pb[bank][:, :], ALU.add).then_inc(*incs[0]))
                    return ("VE", v)

                for it in range(32):
                    items.append((wh["wout"][it, :, :], PCOLS))
                up_phase(16, IT, lambda k: ub_tile(64 + k), wout_evac, banks=(4, 5),
                         rhs_ready=("VE", yz_done))
                x1_done_val[0] = cnt["VE"]
                dbg_dump_vals.append((c, cnt["VE"], cnt["SC"]))
            else:
                x1_done_val[0] = ln1_done

            if STAGE >= 2:
                ln2_done, mat2_vals = ln_phase(c, 2, [("VE", x1_tile_vals[k]) for k in range(HT)])

                for s0 in range(ST):
                    items.append((wh["sw1"][s0, :, :], PCOLS))
                    items.append((wh["sw2"][s0, :, :], PCOLS))

                t_bankwait(4)
                ip = []
                for b in (0, 2, 1, 3):
                    t_bankwait(b)
                    ip.append(use_panel())
                igr = [(4, None), (0, ip[0][0]), (2, ip[1][0]), (1, ip[2][0]), (3, ip[3][0])]
                gvs = [None] * 5
                for k in range(HT):
                    wait("tensor", "VE", mat2_vals[k])
                    for gi, (bank, buf) in enumerate(igr):
                        incs = []
                        if k == HT - 1:
                            gvs[gi] = milestone(incs, "MM")
                        lhsT = gateb[:, k * 8:(k + 1) * 8] if buf is None \
                            else buf[:, k * 128:(k + 1) * 128]
                        out = pb[4][0:8, :] if buf is None else pb[bank][:, :]
                        mm_em(out, lhsT, ln_tile(k), start=(k == 0),
                              stop=(k == HT - 1), incs=incs, skip_gc=True)
                gate_val = gvs[0]
                for pi, g in enumerate((ip[0][1], ip[1][1], ip[2][1], ip[3][1])):
                    panel_done(g, gvs[1 + pi])
                wait("scalar", "MM", gate_val)
                si = []
                lg_val = milestone(si, "SC")
                em("scalar", lambda s, incs=tuple(si):
                   s.copy(lg[:, :], pb[4][0:8, :]).then_inc(*incs[0]))
                bank_free[4] = ("SC", lg_val)

                def sh_evac(s0, bank1, bank2, h1v, h2v):
                    wait("scalar", "MM", h1v)
                    if etmp_war[0] > 0:
                        wait("scalar", "VE", etmp_war[0])
                    si = []
                    sv = milestone(si, "SC")
                    em("scalar", lambda s, b1=bank1, incs=tuple(si):
                       s.activation(etmp[0][:, :], pb[b1][:, :], AF.Silu).then_inc(*incs[0]))
                    wait("vector", "MM", h2v)
                    wait("vector", "SC", sv)
                    vi = []
                    vv = milestone(vi, "VE")
                    em("vector", lambda v, b2=bank2, s0=s0, incs=tuple(vi):
                       v.tensor_tensor(ub_tile(FT + s0), etmp[0][:, :],
                                       pb[b2][:, :], ALU.mult).then_inc(*incs[0]))
                    ub_vals[FT + s0] = vv
                    etmp_war[0] = vv
                    bank_free[bank1] = ("SC", sv)
                    bank_free[bank2] = ("VE", vv)

                def sh_tile(s0):
                    bank1 = s0 % 2
                    bank2 = 2 + s0 % 2
                    buf1, g1 = use_panel()
                    t_bankwait(bank1)
                    for k in range(HT):
                        incs = []
                        if k == HT - 1:
                            h1v = milestone(incs, "MM")
                        mm_em(pb[bank1][:, :], buf1[:, k * 128:(k + 1) * 128],
                              ln_tile(k), start=(k == 0), stop=(k == HT - 1), incs=incs)
                    panel_done(g1, h1v)
                    buf2, g2 = use_panel()
                    t_bankwait(bank2)
                    for k in range(HT):
                        incs = []
                        if k == HT - 1:
                            h2v = milestone(incs, "MM")
                        mm_em(pb[bank2][:, :], buf2[:, k * 128:(k + 1) * 128],
                              ln_tile(k), start=(k == 0), stop=(k == HT - 1), incs=incs)
                    panel_done(g2, h2v)
                    sh_evac(s0, bank1, bank2, h1v, h2v)

                sh_evac(0, 0, 2, gvs[1], gvs[2])
                sh_evac(1, 1, 3, gvs[3], gvs[4])
                sh_tile(2)
                t_bankwait(7)
                wait("tensor", "SC", lg_val)
                for g in range(4):
                    incs = []
                    if g == 3:
                        tp_val = milestone(incs, "MM")
                    def tp_fn(t, g=g, incs=tuple(incs)):
                        ins = t.transpose(pb[7][:, g * 8:(g + 1) * 8],
                                          lg[0:8, g * 128:(g + 1) * 128], idsb[0:8, 0:8])
                        for sem, k2 in incs:
                            ins.then_inc(sem, k2)
                    em("tensor", tp_fn)
                wait("vector", "MM", tp_val)
                vi = []
                tokcp_val = milestone(vi, "VE")
                em("vector", lambda v, incs=tuple(vi):
                   v.tensor_copy(gsc[:, 0:32], pb[7][:, 0:32]).then_inc(*incs[0]))
                wait("vector", "VE", tokcp_val)
                tok = lambda g: gsc[:, g * 8:(g + 1) * 8]
                e_g = lambda g: gsc[:, 32 + g * 8:32 + (g + 1) * 8]
                p_g = lambda g: gsc[:, 64 + g * 8:64 + (g + 1) * 8]
                scr = lambda g: gsc[:, 96 + g * 8:96 + (g + 1) * 8]
                cmb = lambda g: gsc[:, 128 + g * 8:128 + (g + 1) * 8]
                m1n4 = gsc[:, 160:164]
                ssum4 = gsc[:, 192:196]
                lns4 = gsc[:, 200:204]
                rs4 = gsc[:, 204:208]
                m24 = gsc[:, 224:228]
                m1p4 = gsc[:, 256:260]
                m1n = lambda g: gsc[:, 160 + g:161 + g]
                ssum = lambda g: gsc[:, 192 + g:193 + g]
                m2 = lambda g: gsc[:, 224 + g:225 + g]
                m1p = lambda g: gsc[:, 256 + g:257 + g]
                rs = lambda g: gsc[:, 204 + g:205 + g]
                grp3d = lambda base: gsc[:, base:base + 32].rearrange("p (g k) -> p g k", k=8)
                vi = []
                mx_val = milestone(vi, "VE")
                em("vector", lambda v, incs=tuple(vi):
                   v.reduce_max(m1n4, grp3d(0), AX.X).then_inc(*incs[0]))
                wait("vector", "VE", mx_val)
                vi = []
                m1_val = milestone(vi, "VE")
                em("vector", lambda v, incs=tuple(vi):
                   v.tensor_single_scalar(m1n4, m1n4, -1.0, ALU.mult).then_inc(*incs[0]))
                wait("scalar", "VE", m1_val)
                si = []
                exp_val = milestone(si, "SC")
                for g in range(4):
                    incs2 = tuple(si) if g == 3 else ()
                    def exp_fn(s, g=g, incs=incs2):
                        ins = s.activation(e_g(g), tok(g), AF.Exp, bias=m1n(g))
                        for sem, k2 in incs:
                            ins.then_inc(sem, k2)
                    em("scalar", exp_fn)
                wait("vector", "SC", exp_val)
                vi = []
                ssum_val = milestone(vi, "VE")
                em("vector", lambda v, incs=tuple(vi):
                   v.reduce_sum(ssum4, grp3d(32), AX.X).then_inc(*incs[0]))
                wait("scalar", "VE", ssum_val)
                em("scalar", lambda s: s.activation(lns4, ssum4, AF.Ln))
                si = []
                rs_val = milestone(si, "SC")
                em("scalar", lambda s, incs=tuple(si):
                   s.activation(rs4, lns4, AF.Exp, scale=-1.0).then_inc(*incs[0]))
                wait("vector", "SC", rs_val)
                for g in range(4):
                    if g == 3:
                        vi = []
                        pdone_val = milestone(vi, "VE")
                        em("vector", lambda v, g=g, incs=tuple(vi):
                           v.tensor_scalar(p_g(g), e_g(g), rs(g), None, ALU.mult).then_inc(*incs[0]))
                    else:
                        em("vector", lambda v, g=g:
                           v.tensor_scalar(p_g(g), e_g(g), rs(g), None, ALU.mult))
                wait("vector", "VE", pdone_val)
                vi = []
                m1p_val = milestone(vi, "VE")
                em("vector", lambda v, incs=tuple(vi):
                   v.reduce_max(m1p4, grp3d(64), AX.X).then_inc(*incs[0]))
                wait("vector", "VE", m1p_val)
                for g in range(4):
                    if g == 3:
                        vi = []
                        ge1_val = milestone(vi, "VE")
                        em("vector", lambda v, g=g, incs=tuple(vi):
                           v.tensor_scalar(scr(g), p_g(g), m1p(g), None, ALU.is_ge).then_inc(*incs[0]))
                    else:
                        em("vector", lambda v, g=g:
                           v.tensor_scalar(scr(g), p_g(g), m1p(g), None, ALU.is_ge))
                wait("vector", "VE", ge1_val)
                for g in range(4):
                    if g == 3:
                        vi = []
                        pm_val = milestone(vi, "VE")
                        em("vector", lambda v, g=g, incs=tuple(vi):
                           v.tensor_tensor(e_g(g), p_g(g), scr(g), ALU.subtract).then_inc(*incs[0]))
                    else:
                        em("vector", lambda v, g=g:
                           v.tensor_tensor(e_g(g), p_g(g), scr(g), ALU.subtract))
                wait("vector", "VE", pm_val)
                vi = []
                m2_val = milestone(vi, "VE")
                em("vector", lambda v, incs=tuple(vi):
                   v.reduce_max(m24, grp3d(32), AX.X).then_inc(*incs[0]))
                wait("vector", "VE", m2_val)
                for g in range(4):
                    if g == 3:
                        vi = []
                        ge2_val = milestone(vi, "VE")
                        em("vector", lambda v, g=g, incs=tuple(vi):
                           v.tensor_scalar(scr(g), p_g(g), m2(g), None, ALU.is_ge).then_inc(*incs[0]))
                    else:
                        em("vector", lambda v, g=g:
                           v.tensor_scalar(scr(g), p_g(g), m2(g), None, ALU.is_ge))
                wait("vector", "VE", ge2_val)
                for g in range(4):
                    if g == 3:
                        vi = []
                        comb_val = milestone(vi, "VE")
                        em("vector", lambda v, g=g, incs=tuple(vi):
                           v.tensor_tensor(cmb(g), p_g(g), scr(g), ALU.mult).then_inc(*incs[0]))
                    else:
                        em("vector", lambda v, g=g:
                           v.tensor_tensor(cmb(g), p_g(g), scr(g), ALU.mult))
                sh_tile(3)
                sh_tile(4)
                sh_tile(5)
                sh_tile(6)
                t_bankwait(6)
                wait("tensor", "VE", comb_val)
                for g in range(4):
                    incs = []
                    if g == 3:
                        btp_val = milestone(incs, "MM")
                    def btp_fn(t, g=g, incs=tuple(incs)):
                        ins = t.transpose(pb[6][0:8, g * 128:(g + 1) * 128],
                                          cmb(g), idsb[:, 0:128])
                        for sem, k2 in incs:
                            ins.then_inc(sem, k2)
                    em("tensor", btp_fn)
                wait("scalar", "MM", btp_val)
                si = []
                combT_val = milestone(si, "SC")
                em("scalar", lambda s, incs=tuple(si):
                   s.copy(combT[:, :], pb[6][0:8, :]).then_inc(*incs[0]))
                bank_free[6] = ("SC", combT_val)
                bank_free[7] = ("VE", comb_val)
                for s0 in range(7, ST):
                    sh_tile(s0)

                for f in range(FT):
                    items.append((wh["w1"][f, :, :], PCOLS))
                    items.append((wh["w2"][f, :, :], PCOLS))

                combb_vals = {}

                def jit_selector(e):
                    bank = 6 if e % 2 == 0 else 7
                    t_bankwait(bank)
                    wait("tensor", "SC", combT_val)
                    incs = []
                    sv = milestone(incs, "MM")
                    mm_em(pb[bank][0:128, :], selb[0:8, e * 128:(e + 1) * 128], combT[:, :],
                          start=True, stop=True, incs=incs, skip_gc=True)
                    wait("scalar", "MM", sv)
                    if comb_war[e % 2] > 0:
                        wait("scalar", "VE", comb_war[e % 2])
                    si = []
                    cv = milestone(si, "SC")
                    em("scalar", lambda s, e=e, bank=bank, incs=tuple(si):
                       s.copy(combb[e % 2][:, :], pb[bank][:, :]).then_inc(*incs[0]))
                    combb_vals[e] = cv
                    bank_free[bank] = ("SC", cv)

                for f in range(FT):
                    e = f // IET
                    if f % IET == 0:
                        jit_selector(e)
                    bank1 = f % 2
                    bank2 = 2 + f % 2
                    buf1, g1 = use_panel()
                    t_bankwait(bank1)
                    for k in range(HT):
                        incs = []
                        if k == HT - 1:
                            h1v = milestone(incs, "MM")
                        mm_em(pb[bank1][:, :], buf1[:, k * 128:(k + 1) * 128],
                              ln_tile(k), start=(k == 0), stop=(k == HT - 1), incs=incs)
                    panel_done(g1, h1v)
                    buf2, g2 = use_panel()
                    t_bankwait(bank2)
                    for k in range(HT):
                        incs = []
                        if k == HT - 1:
                            h2v = milestone(incs, "MM")
                        mm_em(pb[bank2][:, :], buf2[:, k * 128:(k + 1) * 128],
                              ln_tile(k), start=(k == 0), stop=(k == HT - 1), incs=incs)
                    panel_done(g2, h2v)
                    wait("scalar", "MM", h1v)
                    if etmp_war[0] > 0:
                        wait("scalar", "VE", etmp_war[0])
                    si = []
                    sv = milestone(si, "SC")
                    em("scalar", lambda s, b1=bank1, incs=tuple(si):
                       s.activation(etmp[0][:, :], pb[b1][:, :], AF.Silu).then_inc(*incs[0]))
                    if f == FT - 1:
                        em("scalar", lambda s:
                           s.activation(gsc[0:1, 208:209], epsap[0:1, :], AF.Ln))
                    wait("vector", "MM", h2v)
                    wait("vector", "SC", sv)
                    wait("vector", "SC", combb_vals[e])
                    em("vector", lambda v, b2=bank2:
                       v.tensor_tensor(etmp[0][:, :], etmp[0][:, :], pb[b2][:, :], ALU.mult))
                    vi = []
                    vv = milestone(vi, "VE")
                    em("vector", lambda v, f=f, e=e, incs=tuple(vi):
                       v.tensor_tensor(ub_tile(f), etmp[0][:, :], combb[e % 2][:, :],
                                       ALU.mult).then_inc(*incs[0]))
                    ub_vals[f] = vv
                    etmp_war[0] = vv
                    comb_war[e % 2] = vv
                    bank_free[bank1] = ("SC", sv)
                    bank_free[bank2] = ("VE", vv)
                hprime_done = cnt["VE"]

                for ct in range(16):
                    for b3 in range(6):
                        items.append((wh["d16"][ct * 6 + b3, :, :], PCOLS))
                    items.append((wh["d14"][ct, :, :], 14 * 128))

                nblk = len(DOWN_BLOCKS)
                x2_tile_vals = []
                for ct in range(16):
                    bank = 4 + (ct % 2)
                    t_bankwait(bank)
                    nmm = 0
                    for bi, (kst, klen) in enumerate(DOWN_BLOCKS):
                        buf, g = use_panel()
                        if ct == 0:
                            wait("tensor", "VE",
                                 max(ub_vals[t] for t in range(kst, kst + klen)))
                        for kk in range(klen):
                            gk = kst + kk
                            nmm += 1
                            incs = []
                            if kk == klen - 1:
                                pv = milestone(incs, "MM")
                                if bi == nblk - 1:
                                    gv = pv
                            mm_em(pb[bank][:, :], buf[:, kk * 128:(kk + 1) * 128],
                                  ub_tile(gk), start=(nmm == 1),
                                  stop=(bi == nblk - 1 and kk == klen - 1),
                                  incs=incs)
                        panel_done(g, pv)
                    wait("vector", "MM", gv)
                    vi = []
                    vv = milestone(vi, "VE")
                    x2_tile_vals.append(vv)
                    em("vector", lambda v, ct=ct, bank=bank, incs=tuple(vi):
                       v.tensor_tensor(x0_tile(ct), x0_tile(ct), pb[bank][:, :],
                                       ALU.add).then_inc(*incs[0]))
                    bank_free[bank] = ("VE", vv)
                x2_done_val.append(cnt["VE"])
                final_vals.append(x2_tile_vals)
                if STAGE >= 2:
                    dbg_dump_vals[c] = (c, cnt["VE"], cnt["SC"])
            else:
                x2_done_val.append(x1_done_val[0])
                if STAGE == 1:
                    final_vals.append([x1_tile_vals[o] for o in range(16)])
                else:
                    final_vals.append([ln1_done] * 16)

        n_items = len(items)

        @block.gpsimd
        def _(g):
            g.dma_start(out=gateb[:, :], in_=gate_h[:, :]).then_inc(CDg, 16)
            g.dma_start(out=selb[:, :], in_=sel_h[:, :]).then_inc(CDg, 16)
            g.dma_start(out=onescolb[:, :], in_=ones1_h[:, :]).then_inc(CDg, 16)
            for gidx, (ap, cols) in enumerate(items):
                if gidx >= NBUF:
                    g.wait_ge(MM, pan_mm_vals[gidx - NBUF])
                g.dma_start(out=wb[gidx % NBUF][:, 0:cols], in_=ap).then_inc(WD, 16)

        @block.sync
        def _(sy):
            sy.dma_start(out=dsb[:, :], in_=dvec_h[:, :]).then_inc(CDs, 16)
            sy.dma_start(out=idsb[:, :], in_=ident_h[:, :]).then_inc(CDs, 16)
            sy.dma_start(out=onescol2[:, :], in_=onescol_h[:, :]).then_inc(CDs, 16)
            sy.dma_start(out=onesrow[:, :], in_=onesrow_h[:, :]).then_inc(CDs, 16)
            n_od = 0
            for (st, ln) in OGROUPS:
                sy.dma_start(out=x0[:, st * TC:(st + ln) * TC],
                             in_=x0t_h[0, :, st * TC:(st + ln) * TC]).then_inc(XD, 16)
            for c in range(NCH):
                if DEBUG_DUMP and c < len(dbg_dump_vals):
                    _, vev, scv = dbg_dump_vals[c]
                    sy.wait_ge(VE, vev)
                    sy.wait_ge(SC, scv)
                    sy.dma_start(out=dbgu_h[c, :, :], in_=ub[:, :]).then_inc(OD, 16)
                    sy.dma_start(out=dbgl_h[c, :, :], in_=lnb[:, :]).then_inc(OD, 16)
                    sy.dma_start(out=dbgc_h[c, :, :], in_=combT[:, :]).then_inc(OD, 16)
                    sy.dma_start(out=dbgg_h[c, :, :], in_=gsc[:, :]).then_inc(OD, 16)
                    sy.dma_start(out=dbgs_h[c, :, :], in_=lg[:, :]).then_inc(OD, 16)
                    n_od += 5
                for (st, ln) in OGROUPS:
                    sy.wait_ge(VE, final_vals[c][st + ln - 1])
                    sy.dma_start(out=out_h[c, :, st * TC:(st + ln) * TC],
                                 in_=x0[:, st * TC:(st + ln) * TC]).then_inc(OD, 16)
                    n_od += 1
                    if c + 1 < NCH:
                        sy.dma_start(out=x0[:, st * TC:(st + ln) * TC],
                                     in_=x0t_h[c + 1, :, st * TC:(st + ln) * TC]).then_inc(XD, 16)
            sy.wait_ge(OD, 16 * n_od)

        @block.tensor
        def _(t):
            for fn in sched["tensor"]:
                fn(t)

        @block.scalar
        def _(s):
            for fn in sched["scalar"]:
                fn(s)

        @block.vector
        def _(v):
            for fn in sched["vector"]:
                fn(v)

    return nc



def _up_panels(W):
    O, Kh = W.shape
    K = Kh // 128
    nkb = K // 16
    A = W.reshape(O // 128, 128, nkb, 16, 128)
    return np.ascontiguousarray(A.transpose(0, 2, 4, 3, 1)).reshape(-1, 128, 16 * 128)


def kernel(L0_attn_hidden_states, L0_attn_W_in, L0_attn_conv_w, L0_attn_W_x,
           L0_attn_W_dt, L0_attn_b_dt, L0_attn_D, L0_attn_W_out,
           L0_ffn_gate_w, L0_ffn_w1, L0_ffn_w2, L0_ffn_w3,
           L0_ffn_sw1, L0_ffn_sw2, L0_ffn_sw3):
    f32 = np.float32
    x = np.ascontiguousarray(np.asarray(L0_attn_hidden_states, dtype=f32).reshape(16384, 2048))
    W_in = np.asarray(L0_attn_W_in, f32)
    conv_w = np.asarray(L0_attn_conv_w, f32)
    D = np.asarray(L0_attn_D, f32)
    W_out = np.asarray(L0_attn_W_out, f32)
    gate_w = np.asarray(L0_ffn_gate_w, f32)
    w1 = np.asarray(L0_ffn_w1, f32)
    w2 = np.asarray(L0_ffn_w2, f32)
    w3 = np.asarray(L0_ffn_w3, f32)
    sw1 = np.asarray(L0_ffn_sw1, f32)
    sw2 = np.asarray(L0_ffn_sw2, f32)
    sw3 = np.asarray(L0_ffn_sw3, f32)

    weights = {
        "win": _up_panels(W_in),
        "conv": _up_panels(conv_w),
        "wout": _up_panels(W_out),
        "w1": _up_panels(w1.reshape(8 * 1408, 2048)),
        "w2": _up_panels(w2.reshape(8 * 1408, 2048)),
        "sw1": _up_panels(sw1),
        "sw2": _up_panels(sw2),
        "gate": np.ascontiguousarray(gate_w.reshape(8, 16, 128).transpose(2, 1, 0)).reshape(128, 128),
        "dvec": np.ascontiguousarray(D.reshape(32, 128).T),
        "ident": np.eye(128, dtype=f32),
        "ones1": np.ones((128, 1), f32),
        "onescol": np.concatenate([np.ones((128, 1), f32),
                                   np.full((128, 1), EPS, f32)], axis=1),
        "onesrow": np.ones((1, 128), f32),
    }
    Wd = np.concatenate([np.ascontiguousarray(w3.transpose(0, 2, 1)).reshape(8 * 1408, 2048),
                         np.ascontiguousarray(sw3.T)], axis=0)
    A = Wd.reshape(110, 128, 16, 128)
    d16 = np.empty((96, 128, PCOLS), f32)
    d14 = np.empty((16, 128, 14 * 128), f32)
    for ct in range(16):
        for b in range(6):
            d16[ct * 6 + b] = A[b * 16:(b + 1) * 16, :, ct, :].transpose(1, 0, 2).reshape(128, PCOLS)
        d14[ct] = A[96:110, :, ct, :].transpose(1, 0, 2).reshape(128, 14 * 128)
    weights["d16"] = np.ascontiguousarray(d16)
    weights["d14"] = np.ascontiguousarray(d14)
    sel = np.zeros((8, 1024), f32)
    for e in range(8):
        sel[e, e * 128:(e + 1) * 128] = 1.0
    weights["sel"] = sel

    nc = build_program()

    in_maps = []
    for core in range(NCORE):
        xc = x[core * TCORE:(core + 1) * TCORE, :]
        x0t = np.ascontiguousarray(
            xc.reshape(NCH, TC, HT, 128).transpose(0, 3, 2, 1)).reshape(NCH, 128, HT * TC)
        in_maps.append({**weights, "x0t": x0t})

    res = run_bass_kernel_spmd(nc, in_maps, core_ids=list(range(NCORE)))

    out = np.empty((16384, 2048), f32)
    for core in range(NCORE):
        o = np.asarray(res.results[core]["out"])
        oc = o.reshape(NCH, 128, HT, TC).transpose(0, 3, 2, 1).reshape(TCORE, 2048)
        out[core * TCORE:(core + 1) * TCORE, :] = oc
    return out.reshape(1, 16384, 2048)


if __name__ == "__main__":
    pass
